# version 07
# baseline (speedup 1.0000x reference)
import numpy as np

import concourse.bass as bass
import concourse.mybir as mybir
from concourse.bass_utils import run_bass_kernel_spmd

F32 = mybir.dt.float32
BF16 = mybir.dt.bfloat16
AF = mybir.ActivationFunctionType
try:
    ALU = mybir.AluOpType
except AttributeError:
    from concourse.alu_op_type import AluOpType as ALU
AX = mybir.AxisListType

D = 1024
T = 2048
TH = 1024
E2 = 2048
H4 = 4096
NE = 9
NF = 512
SEG = 256
NSEG_H = 4


class Prog:
    ENGS = ("T", "V", "S", "Q", "G")

    def __init__(self):
        self.ops = {e: [] for e in self.ENGS}
        self.need = set()
        self.last_w = {}
        self.readers = {}
        self.gdeps = []

    def do(self, eng, fn, reads=(), writes=(), deps=(), hz=False):
        dset = list(deps) + list(self.gdeps)
        for b in reads:
            t = self.last_w.get(b)
            if t is not None:
                dset.append(t)
        for b in writes:
            t = self.last_w.get(b)
            if t is not None:
                dset.append(t)
            dset.extend(self.readers.get(b, ()))
        dset = [d for d in dset if d is not None]
        self.ops[eng].append((tuple(dset), fn, hz))
        tok = (eng, len(self.ops[eng]) - 1)
        for d in dset:
            if d[0] != eng:
                self.need.add(d)
        for b in reads:
            self.readers.setdefault(b, []).append(tok)
        for b in writes:
            self.last_w[b] = tok
            self.readers[b] = []
        return tok

    def barrier(self):
        toks = []
        for e in self.ENGS:
            if self.ops[e]:
                toks.append((e, len(self.ops[e]) - 1))
        self.gdeps = toks
        for t in toks:
            self.need.add(t)

    def realize(self, nc):
        for e in self.ENGS:
            if self.ops[e]:
                self.need.add((e, len(self.ops[e]) - 1))
        for e in self.ENGS:
            for deps, _fn, hz in self.ops[e]:
                if hz:
                    for d in deps:
                        if d[0] == e:
                            self.need.add(d)
        vals = {}
        finals = {}
        sems = {}
        cms = []
        for e in self.ENGS:
            cm = nc.semaphore("sem_" + e)
            sems[e] = cm.__enter__()
            cms.append(cm)
            acc = 0
            inc = 16 if e in ("Q", "G") else 1
            always = e in ("Q", "G")
            for i in range(len(self.ops[e])):
                if always or (e, i) in self.need:
                    acc += inc
                vals[(e, i)] = acc
            finals[e] = acc

        block_cm = nc.Block()
        block = block_cm.__enter__()

        def make_runner(e):
            inc = 16 if e in ("Q", "G") else 1
            always = e in ("Q", "G")

            def run(engine):
                watermark = {o: 0 for o in self.ENGS}
                for i, (deps, fn, hz) in enumerate(self.ops[e]):
                    for d in deps:
                        de = d[0]
                        if de == e and not hz:
                            continue
                        v = vals[d]
                        if v > watermark[de]:
                            engine.wait_ge(sems[de], v)
                            watermark[de] = v
                    ins = fn(engine)
                    if always or (e, i) in self.need:
                        ins.then_inc(sems[e], inc)
                if e == "G":
                    for o in self.ENGS:
                        if finals[o] > 0 and o != "G":
                            engine.wait_ge(sems[o], finals[o])
                    if finals["G"] > 0:
                        engine.wait_ge(sems["G"], finals["G"])
            return run

        block.tensor(make_runner("T"))
        block.vector(make_runner("V"))
        block.scalar(make_runner("S"))
        block.sync(make_runner("Q"))
        block.gpsimd(make_runner("G"))
        block_cm.__exit__(None, None, None)
        for cm in cms:
            cm.__exit__(None, None, None)


def build_nc():
    nc = bass.Bass("TRN2", target_bir_lowering=False)
    p = Prog()

    dx = nc.dram_tensor("xT", [D, T], F32, kind="ExternalInput").ap()
    dwin = nc.dram_tensor("w_inT", [D, E2], F32, kind="ExternalInput").ap()
    dbin = nc.dram_tensor("b_in", [1, E2], F32, kind="ExternalInput").ap()
    dwout = nc.dram_tensor("w_outT", [E2, D], F32, kind="ExternalInput").ap()
    dbout = nc.dram_tensor("b_out", [1, D], F32, kind="ExternalInput").ap()
    dwpad = nc.dram_tensor("wpad", [NF, E2], F32, kind="ExternalInput").ap()
    dbab = nc.dram_tensor("b_ab", [1, E2], F32, kind="ExternalInput").ap()
    dwfc = nc.dram_tensor("WfcA", [NE, D, H4], F32, kind="ExternalInput").ap()
    dwpr = nc.dram_tensor("WprA", [NE, H4, D], F32, kind="ExternalInput").ap()
    dbfc = nc.dram_tensor("bfc9", [128, NE * 32], F32, kind="ExternalInput").ap()
    dbpr = nc.dram_tensor("bprj", [8, D], F32, kind="ExternalInput").ap()
    dbps = nc.dram_tensor("bprjs", [1, D], F32, kind="ExternalInput").ap()
    dwr = nc.dram_tensor("wr", [D, 8], F32, kind="ExternalInput").ap()
    dcv = nc.dram_tensor("cvec", [128, 8], F32, kind="ExternalInput").ap()
    dwmod = nc.dram_tensor("wmod", [D, 6 * D], F32, kind="ExternalInput").ap()
    defr = nc.dram_tensor("EfRe", [NF, NF], F32, kind="ExternalInput").ap()
    defi = nc.dram_tensor("EfIm", [NF, NF], F32, kind="ExternalInput").ap()
    deir = nc.dram_tensor("EiRe", [NF, SEG], F32, kind="ExternalInput").ap()
    deii = nc.dram_tensor("EiIm", [NF, SEG], F32, kind="ExternalInput").ap()
    dones = nc.dram_tensor("onesb", [128, 512], BF16, kind="ExternalInput").ap()
    did = nc.dram_tensor("idf", [128, 128], F32, kind="ExternalInput").ap()
    dsel = nc.dram_tensor("selm", [8, 8 * 128], BF16, kind="ExternalInput").ap()
    dout = nc.dram_tensor("out", [T, D], F32, kind="ExternalOutput").ap()

    ctxs = []

    def sb(name, shape, dt):
        cm = nc.sbuf_tensor(name, shape, dt)
        t = cm.__enter__()
        ctxs.append(cm)
        return t

    def ps(name, shape):
        cm = nc.psum_tensor(name, shape, F32)
        t = cm.__enter__()
        ctxs.append(cm)
        return t

    arena = sb("arena", [128, 40960], BF16)
    AZ = 0
    AU = 8192
    A_H = AU
    A_X = AU + 16384
    A_Y = AU + 20480
    A_CAT = AU + 24576
    A_WIN = AU + 28672
    A_WOUT = AU + 29696
    A_STG = AU + 31744
    M_WFC = AU
    M_WPR = AU + 8192
    M_H1 = AU + 16384
    M_GBC = AU + 20480

    xres = sb("xres", [128, 8 * 1024], F32)
    hbuf = sb("hbuf", [128, 8 * 1024], BF16)
    stgF = sb("stgF", [128, 2 * 1024], F32)
    f32s = sb("f32s", [128, 2048], F32)
    xb2 = sb("xb2", [128, 2 * 512], BF16)
    xq2 = sb("xq2", [128, 2 * 512], BF16)
    rowsF = sb("rowsF", [1, 2048], F32)
    rowsB = sb("rowsB", [1, 1024], BF16)
    binrow = sb("binrow", [1, E2], BF16)
    babrow = sb("babrow", [1, E2], BF16)
    bo_row = sb("bo_row", [1, 2 * D], BF16)
    efre = sb("efre", [128, 2048], BF16)
    efim = sb("efim", [128, 2048], BF16)
    eire = sb("eire", [128, 1024], BF16)
    eiim = sb("eiim", [128, 1024], BF16)
    onest = sb("onest", [128, 512], BF16)
    idf = sb("idf_s", [128, 128], F32)
    selm = sb("selm_s", [8, 8 * 128], BF16)
    bprj = sb("bprj_s", [8, D], BF16)
    gfmb = sb("gfmb", [8, TH], BF16)
    bfc9 = sb("bfc9_s", [128, NE * 32], F32)
    wrb = sb("wrb", [128, 64], BF16)
    modv = sb("modv", [128, 48], F32)
    gax = sb("gax", [128, 64], F32)
    lgt = sb("lgt", [128, 32], F32)
    cfm = sb("cfm", [128, 8], F32)
    imsk = sb("imsk", [128, 8], mybir.dt.int32)
    scb = sb("scb", [128, 8], BF16)
    r512 = sb("r512", [1, 512], F32)

    pb = [ps("p%d" % i, [128, 512]) for i in range(8)]

    def dma(q, out_ap, in_ap, reads=(), writes=(), deps=()):
        return p.do(q, lambda e, o=out_ap, i=in_ap: e.dma_start(out=o, in_=i),
                    reads=reads, writes=writes, deps=deps)

    def mm(out_ap, lt, rt, st, sp, reads=(), writes=(), deps=()):
        return p.do("T", lambda e, o=out_ap, l=lt, r=rt, s=st, x=sp:
                    e.matmul(o, l, r, start=s, stop=x),
                    reads=reads, writes=writes, deps=deps)

    def act(out_ap, in_ap, func, bias=0.0, scale=1.0, reads=(), writes=(), deps=()):
        return p.do("S", lambda e, o=out_ap, i=in_ap, f=func, b=bias, s=scale:
                    e.activation(o, i, f, bias=b, scale=s),
                    reads=reads, writes=writes, deps=deps)

    def vtt(out_ap, a_ap, b_ap, op, reads=(), writes=(), deps=(), hz=False):
        return p.do("V", lambda e, o=out_ap, a=a_ap, b=b_ap, q=op:
                    e.tensor_tensor(o, a, b, q),
                    reads=reads, writes=writes, deps=deps, hz=hz)

    def vts(out_ap, a_ap, s1, op, reads=(), writes=(), deps=(), hz=False):
        return p.do("V", lambda e, o=out_ap, a=a_ap, s=s1, q=op:
                    e.tensor_scalar(o, a, s, None, q),
                    reads=reads, writes=writes, deps=deps, hz=hz)

    def vstt(out_ap, in0, sc, in1, op0, op1, reads=(), writes=(), deps=(), hz=False):
        return p.do("V", lambda e, o=out_ap, i0=in0, s=sc, i1=in1, a=op0, b=op1:
                    e.scalar_tensor_tensor(o, i0, s, i1, a, b),
                    reads=reads, writes=writes, deps=deps, hz=hz)

    def stg(sl):
        return stgF[:, sl * 1024:(sl + 1) * 1024]

    def stg5(sl):
        return stgF[:, sl * 1024: sl * 1024 + 512]

    dma("G", onest[:, :], dones[:, :], writes=[("ones",)])
    dma("G", idf[:, :], did[:, :], writes=[("idf",)])
    dma("G", selm[:, :], dsel[:, :], writes=[("selm",)])
    dma("G", bfc9[:, :], dbfc[:, :], writes=[("bfc9",)])
    dma("G", cfm[:, :], dcv[:, :], writes=[("cfm",)])
    for nm, dr, dst, w in (("efre", defr, efre, 2048), ("efim", defi, efim, 2048),
                           ("eire", deir, eire, 1024), ("eiim", deii, eiim, 1024)):
        for c in range(4):
            sl = c % 2
            st = stg5(sl) if w == 2048 else stgF[:, sl * 1024: sl * 1024 + 256]
            dma("G", st, dr[c * 128:(c + 1) * 128, :], writes=[("stgF", sl)])
            act(dst[:, c * (w // 4):(c + 1) * (w // 4)], st, AF.Copy,
                reads=[("stgF", sl)], writes=[(nm,)])
    for nm, dr, dst in (("binrow", dbin, binrow), ("babrow", dbab, babrow)):
        dma("G", stgF[0:1, 0:2048], dr[:, :], writes=[("stgF", 0), ("stgF", 1)])
        act(dst[0:1, :], stgF[0:1, 0:2048], AF.Copy,
            reads=[("stgF", 0), ("stgF", 1)], writes=[(nm,)])
    dma("G", stgF[0:1, 0:1024], dbout[:, :], writes=[("stgF", 0)])
    act(bo_row[0:1, 0:1024], stgF[0:1, 0:1024], AF.Copy,
        reads=[("stgF", 0)], writes=[("borow",)])
    dma("G", stgF[0:1, 1024:2048], dbps[:, :], writes=[("stgF", 1)])
    act(bo_row[0:1, 1024:2048], stgF[0:1, 1024:2048], AF.Copy,
        reads=[("stgF", 1)], writes=[("borow",)])
    dma("G", stgF[0:8, 0:1024], dbpr[:, :], writes=[("stgF", 0)])
    p.do("V", lambda e: e.tensor_copy(bprj[:, :], stgF[0:8, 0:1024]),
         reads=[("stgF", 0)], writes=[("bprj",)])
    for k in range(8):
        sl = k % 2
        dma("G", stgF[:, sl * 1024: sl * 1024 + 8], dwr[k * 128:(k + 1) * 128, :],
            writes=[("stgF", sl)])
        act(wrb[:, k * 8:(k + 1) * 8], stgF[:, sl * 1024: sl * 1024 + 8], AF.Copy,
            reads=[("stgF", sl)], writes=[("wrb",)])

    p.do("V", lambda e: e.memset(arena[:, AZ + 2 * 2048: AZ + 4 * 2048], 0.0),
         writes=[("zwin", 2), ("zwin", 3)])

    act(cfm[:, :], cfm[:, :], AF.Silu, reads=[("cfm",)], writes=[("cfm",)])
    p.do("V", lambda e: e.tensor_copy(scb[:, :], cfm[:, :]),
         reads=[("cfm",)], writes=[("scb",)])
    for j in range(12):
        for k in range(8):
            sl = k % 2
            dma("G", stg5(sl), dwmod[k * 128:(k + 1) * 128, j * 512:(j + 1) * 512],
                writes=[("stgF", sl)])
            bslc = arena[:, A_STG + sl * 512: A_STG + (sl + 1) * 512]
            act(bslc, stg5(sl), AF.Copy, reads=[("stgF", sl)], writes=[("stgB", sl)])
            mm(pb[6][0:1, :], scb[:, k:k + 1], bslc, k == 0, k == 7,
               reads=[("scb",), ("stgB", sl)], writes=[("pb", 6)])
        act(r512[0:1, :], pb[6][0:1, :], AF.Copy,
            reads=[("pb", 6)], writes=[("r512",)])
        for b4 in range(4):
            mcol = j * 4 + b4
            p.do("T", lambda e, b=b4: e.transpose(
                pb[7][:, 0:1], r512[0:1, b * 128:(b + 1) * 128], idf[0:1, 0:1]),
                reads=[("r512",), ("idf",)], writes=[("pb", 7)])
            v, kd = mcol // 8, mcol % 8
            act(modv[:, v * 8 + kd: v * 8 + kd + 1], pb[7][:, 0:1], AF.Copy,
                reads=[("pb", 7)], writes=[("modv",)])
    SA, CA, GA, SB, CB, GB = 0, 8, 16, 24, 32, 40
    vts(gax[:, 0:8], modv[:, CA:CA + 8], 1.0, ALU.add,
        reads=[("modv",)], writes=[("gax",)])
    vts(gax[:, 8:16], modv[:, CB:CB + 8], 1.0, ALU.add,
        reads=[("modv",)], writes=[("gax",)])
    for base, off in ((GA, 16), (GB, 24)):
        vts(lgt[:, 0:8], modv[:, base:base + 8], 0.0, ALU.is_ge,
            reads=[("modv",)], writes=[("lgt",)], hz=True)
        vts(lgt[:, 8:16], modv[:, base:base + 8], 0.25, ALU.max,
            reads=[("modv",)], writes=[("lgt2",)], hz=True)
        vts(lgt[:, 16:24], modv[:, base:base + 8], -0.25, ALU.min,
            reads=[("modv",)], writes=[("lgt3",)], hz=True)
        vtt(lgt[:, 24:32], lgt[:, 8:16], lgt[:, 16:24], ALU.subtract,
            reads=[("lgt2",), ("lgt3",)], writes=[("lgt4",)], hz=True)
        vtt(lgt[:, 8:16], lgt[:, 24:32], lgt[:, 0:8], ALU.mult,
            reads=[("lgt4",), ("lgt",)], writes=[("lgt2",)], hz=True)
        vtt(gax[:, off:off + 8], lgt[:, 8:16], lgt[:, 16:24], ALU.add,
            reads=[("lgt2",), ("lgt3",)], writes=[("gax",)], hz=True)
    C1c = lambda d: gax[:, d:d + 1]
    C2c = lambda d: gax[:, 8 + d:8 + d + 1]
    GAc = lambda d: gax[:, 16 + d:16 + d + 1]
    GBc = lambda d: gax[:, 24 + d:24 + d + 1]
    SAc = lambda d: modv[:, SA + d:SA + d + 1]
    SBc = lambda d: modv[:, SB + d:SB + d + 1]

    p.barrier()

    def layer_norm(tq, Cc, Sc):
        q0 = tq * 512
        for k in range(8):
            sl = k % 2
            xs = xres[:, k * 1024 + q0: k * 1024 + q0 + 512]
            act(xb2[:, sl * 512:(sl + 1) * 512], xs, AF.Copy,
                reads=[("x", k)], writes=[("xb", sl)])
            act(xq2[:, sl * 512:(sl + 1) * 512], xs, AF.Square,
                reads=[("x", k)], writes=[("xq", sl)])
            mm(pb[6][0:1, :], onest[:, 0:1], xb2[:, sl * 512:(sl + 1) * 512],
               k == 0, k == 7, reads=[("xb", sl), ("ones",)], writes=[("pb", 6)])
            mm(pb[7][0:1, :], onest[:, 0:1], xq2[:, sl * 512:(sl + 1) * 512],
               k == 0, k == 7, reads=[("xq", sl), ("ones",)], writes=[("pb", 7)])
        act(rowsF[0:1, 0:512], pb[6][0:1, :], AF.Copy, scale=1.0 / 1024,
            reads=[("pb", 6)], writes=[("rm",)])
        act(rowsF[0:1, 512:1024], pb[7][0:1, :], AF.Copy, scale=1.0 / 1024,
            reads=[("pb", 7)], writes=[("rq",)])
        vtt(rowsF[0:1, 1024:1536], rowsF[0:1, 0:512], rowsF[0:1, 0:512],
            ALU.mult, reads=[("rm",)], writes=[("rv",)])
        vstt(rowsF[0:1, 1024:1536], rowsF[0:1, 1024:1536], -1.0,
             rowsF[0:1, 512:1024], ALU.mult, ALU.add,
             reads=[("rq",)], writes=[("rv",)])
        vts(rowsF[0:1, 1024:1536], rowsF[0:1, 1024:1536], 1e-5, ALU.add,
            reads=[("rv",)], writes=[("rv",)])
        act(rowsF[0:1, 1024:1536], rowsF[0:1, 1024:1536], AF.Sqrt,
            reads=[("rv",)], writes=[("rv",)])
        p.do("V", lambda e: e.reciprocal(rowsF[0:1, 1024:1536],
                                         rowsF[0:1, 1024:1536]),
             reads=[], writes=[("rv",)])
        vtt(rowsF[0:1, 1536:2048], rowsF[0:1, 0:512], rowsF[0:1, 1024:1536],
            ALU.mult, reads=[("rm",), ("rv",)], writes=[("rqq",)])
        vts(rowsF[0:1, 1536:2048], rowsF[0:1, 1536:2048], -1.0, ALU.mult,
            reads=[], writes=[("rqq",)])
        p.do("V", lambda e: e.tensor_copy(rowsB[0:1, 0:512], rowsF[0:1, 1024:1536]),
             reads=[("rv",)], writes=[("rbp",)])
        p.do("V", lambda e: e.tensor_copy(rowsB[0:1, 512:1024], rowsF[0:1, 1536:2048]),
             reads=[("rqq",)], writes=[("rbq",)])
        mm(pb[4][:, :], onest[0:1, 0:128], rowsB[0:1, 0:512], True, True,
           reads=[("rbp",), ("ones",)], writes=[("pb", 4)])
        mm(pb[5][:, :], onest[0:1, 0:128], rowsB[0:1, 512:1024], True, True,
           reads=[("rbq",), ("ones",)], writes=[("pb", 5)])
        act(f32s[:, 0:512], pb[4][:, :], AF.Copy, reads=[("pb", 4)], writes=[("Pbc",)])
        act(f32s[:, 512:1024], pb[5][:, :], AF.Copy, reads=[("pb", 5)], writes=[("Qbc",)])
        for d in range(8):
            xs = xres[:, d * 1024 + q0: d * 1024 + q0 + 512]
            vtt(f32s[:, 1024:1536], xs, f32s[:, 0:512], ALU.mult,
                reads=[("x", d), ("Pbc",)], writes=[("t1",)])
            vtt(f32s[:, 1024:1536], f32s[:, 1024:1536], f32s[:, 512:1024],
                ALU.add, reads=[("Qbc",)], writes=[("t1",)])
            p.do("V", lambda e, dd=d, o=hbuf[:, d * 1024 + q0: d * 1024 + q0 + 512],
                 s1=Cc(d), s2=Sc(d): e.tensor_scalar(
                     o, f32s[:, 1024:1536], s1, s2, ALU.mult, ALU.add),
                 reads=[("t1",), ("modv",), ("gax",)], writes=[("h", d, tq)])

    for half in range(2):
        t0 = half * TH
        for d in range(8):
            dma("G", xres[:, d * 1024:(d + 1) * 1024],
                dx[d * 128:(d + 1) * 128, t0:t0 + TH], writes=[("x", d)])

        for kq in range(2):
            sl = kq % 2
            dma("Q", stg(sl), dwpad[kq * 128:(kq + 1) * 128, 0:1024],
                writes=[("stgF", sl)])
            act(arena[:, A_Y + kq * 4096: A_Y + kq * 4096 + 1024], stg(sl), AF.Copy,
                reads=[("stgF", sl)], writes=[("wpb", kq, 0)])
            dma("Q", stg(sl), dwpad[kq * 128:(kq + 1) * 128, 1024:2048],
                writes=[("stgF", sl)])
            act(arena[:, A_Y + kq * 4096 + 1024: A_Y + kq * 4096 + 2048], stg(sl),
                AF.Copy, reads=[("stgF", sl)], writes=[("wpb", kq, 1)])
        for rim in range(2):
            ef = efre if rim == 0 else efim
            hof = A_H + rim * 8192
            for fc in range(4):
                for cs in range(4):
                    pi = (fc * 4 + cs) % 2
                    pp = pb[pi]
                    for kq in range(2):
                        mm(pp[:, :],
                           ef[:, kq * 512 + fc * 128: kq * 512 + (fc + 1) * 128],
                           arena[:, A_Y + kq * 4096 + cs * 512:
                                 A_Y + kq * 4096 + (cs + 1) * 512],
                           kq == 0, kq == 1,
                           reads=[("wpb", kq, cs // 2), (("efre", "efim")[rim],)],
                           writes=[("pb", pi)])
                    act(arena[:, hof + fc * 2048 + cs * 512:
                              hof + fc * 2048 + (cs + 1) * 512], pp[:, :], AF.Copy,
                        reads=[("pb", pi)], writes=[("H", rim, fc)])

        for tq in range(2):
            layer_norm(tq, C1c, SAc)

        p.barrier()

        for s in range(NSEG_H):
            sg = half * NSEG_H + s
            for es in range(4):
                pi0, pi1 = (0, 1) if es % 2 == 0 else (2, 3)
                for k in range(8):
                    sl = k % 2
                    dma("Q", stg5(sl), dwin[k * 128:(k + 1) * 128,
                                            es * 512:(es + 1) * 512],
                        writes=[("stgF", sl)])
                    wslc = arena[:, A_WIN + sl * 512: A_WIN + (sl + 1) * 512]
                    act(wslc, stg5(sl), AF.Copy, reads=[("stgF", sl)],
                        writes=[("winb", sl)])
                    for tb in range(2):
                        tbl = s * 2 + tb
                        mm(pb[(pi0, pi1)[tb]][:, :],
                           hbuf[:, k * 1024 + tbl * 128: k * 1024 + (tbl + 1) * 128],
                           wslc, k == 0, False,
                           reads=[("winb", sl), ("h", k, tbl // 4)],
                           writes=[("pb", (pi0, pi1)[tb])])
                for tb in range(2):
                    tbl = s * 2 + tb
                    slot = (2 * sg + tb) % 4
                    pp = pb[(pi0, pi1)[tb]]
                    mm(pp[:, :], onest[0:1, 0:128],
                       binrow[0:1, es * 512:(es + 1) * 512], False, True,
                       reads=[("binrow",), ("ones",)], writes=[("pb", (pi0, pi1)[tb])])
                    act(arena[:, AZ + slot * 2048 + es * 512:
                              AZ + slot * 2048 + (es + 1) * 512], pp[:, :], AF.Copy,
                        reads=[("pb", (pi0, pi1)[tb])], writes=[("zwin", slot, es)])

            for cs in range(2):
                for cv in range(2):
                    cbase = cv * 1024 + cs * 512
                    for fc in range(4):
                        for rim in range(2):
                            pi = 2 + (2 * fc + rim) % 4
                            pp = pb[pi]
                            ef = efre if rim == 0 else efim
                            for q in range(4):
                                slq = (2 * sg - 2 + q) % 4
                                mm(pp[:, :],
                                   ef[:, q * 512 + fc * 128: q * 512 + (fc + 1) * 128],
                                   arena[:, AZ + slq * 2048 + cbase:
                                         AZ + slq * 2048 + cbase + 512],
                                   q == 0, q == 3,
                                   reads=[("zwin", slq, cbase // 512),
                                          ("zwin", slq, cbase // 512 + 0),
                                          (("efre", "efim")[rim],)],
                                   writes=[("pb", pi)])
                            act(arena[:, A_X + (2 * fc + rim) * 512:
                                      A_X + (2 * fc + rim + 1) * 512],
                                pp[:, :], AF.Copy,
                                reads=[("pb", pi)], writes=[("X", fc, rim)])
                    for fc in range(4):
                        xr = arena[:, A_X + (2 * fc) * 512: A_X + (2 * fc + 1) * 512]
                        xi = arena[:, A_X + (2 * fc + 1) * 512: A_X + (2 * fc + 2) * 512]
                        yr = arena[:, A_Y + (2 * fc) * 512: A_Y + (2 * fc + 1) * 512]
                        yi = arena[:, A_Y + (2 * fc + 1) * 512: A_Y + (2 * fc + 2) * 512]
                        tm = arena[:, A_STG + 512: A_STG + 1024]
                        hr = arena[:, A_H + fc * 2048 + cbase: A_H + fc * 2048 + cbase + 512]
                        hi = arena[:, A_H + 8192 + fc * 2048 + cbase:
                                   A_H + 8192 + fc * 2048 + cbase + 512]
                        vtt(yr, xr, hr, ALU.mult,
                            reads=[("X", fc, 0), ("H", 0, fc)], writes=[("Y", fc, 0)])
                        vtt(tm, xi, hi, ALU.mult,
                            reads=[("X", fc, 1), ("H", 1, fc)], writes=[("tm",)])
                        vtt(yr, yr, tm, ALU.subtract,
                            reads=[("tm",)], writes=[("Y", fc, 0)])
                        vtt(yi, xr, hi, ALU.mult,
                            reads=[("X", fc, 0), ("H", 1, fc)], writes=[("Y", fc, 1)])
                        vtt(tm, xi, hr, ALU.mult,
                            reads=[("X", fc, 1), ("H", 0, fc)], writes=[("tm",)])
                        vtt(yi, yi, tm, ALU.add,
                            reads=[("tm",)], writes=[("Y", fc, 1)])
                    for ci in range(4):
                        pbank = (0, 1, 6, 7)[ci] if cv == 0 else (2, 3, 4, 5)[ci]
                        pp = pb[pbank]
                        first = True
                        for fc in range(4):
                            for rim in range(2):
                                ysl = arena[:, A_Y + (2 * fc + rim) * 512 + ci * 128:
                                            A_Y + (2 * fc + rim) * 512 + (ci + 1) * 128]
                                ei = eire if rim == 0 else eiim
                                mm(pp[:, 0:256], ysl, ei[:, fc * 256:(fc + 1) * 256],
                                   first, False,
                                   reads=[("Y", fc, rim), (("eire", "eiim")[rim],)],
                                   writes=[("pb", pbank)])
                                first = False
                        mm(pp[:, 0:256],
                           babrow[0:1, cbase + ci * 128: cbase + (ci + 1) * 128],
                           onest[0:1, 0:256], False, True,
                           reads=[("babrow",), ("ones",)], writes=[("pb", pbank)])
                for ci in range(4):
                    ba, bb_ = (0, 1, 6, 7)[ci], (2, 3, 4, 5)[ci]
                    pa, pc = pb[ba], pb[bb_]
                    ka, kb = ("pb", ba), ("pb", bb_)
                    sa_t = arena[:, A_STG: A_STG + 256]
                    sb_t = arena[:, A_STG + 256: A_STG + 512]
                    act(sa_t, pa[:, 0:256], AF.Sigmoid, reads=[ka], writes=[("siga",)])
                    act(sb_t, pc[:, 0:256], AF.Sigmoid, reads=[kb], writes=[("sigb",)])
                    eca = cs * 4 + ci
                    vtt(arena[:, A_CAT + eca * 256: A_CAT + (eca + 1) * 256],
                        pa[:, 0:256], sb_t, ALU.mult,
                        reads=[ka, ("sigb",)], writes=[("cat", eca)])
                    vtt(arena[:, A_CAT + (8 + eca) * 256: A_CAT + (9 + eca) * 256],
                        pc[:, 0:256], sa_t, ALU.mult,
                        reads=[kb, ("siga",)], writes=[("cat", 8 + eca)])

            for ec in range(16):
                sl = ec % 2
                dma("Q", stg(sl), dwout[ec * 128:(ec + 1) * 128, :],
                    writes=[("stgF", sl)])
                wsl = arena[:, A_WOUT + sl * 1024: A_WOUT + (sl + 1) * 1024]
                act(wsl, stg(sl), AF.Copy, reads=[("stgF", sl)], writes=[("woutb", sl)])
                for h in range(8):
                    mm(pb[h][:, 0:256], wsl[:, h * 128:(h + 1) * 128],
                       arena[:, A_CAT + ec * 256: A_CAT + (ec + 1) * 256],
                       ec == 0, False,
                       reads=[("woutb", sl), ("cat", ec)], writes=[("pb", h)])
            for h in range(8):
                mm(pb[h][:, 0:256], bo_row[0:1, h * 128:(h + 1) * 128],
                   onest[0:1, 0:256], False, True,
                   reads=[("borow",), ("ones",)], writes=[("pb", h)])
                xs = xres[:, h * 1024 + s * 256: h * 1024 + (s + 1) * 256]
                vstt(xs, pb[h][:, 0:256], GAc(h), xs, ALU.mult, ALU.add,
                     reads=[("pb", h), ("gax",)], writes=[("x", h)])

        p.barrier()

        for tq in range(2):
            layer_norm(tq, C2c, SBc)

        for tb in range(8):
            for k in range(8):
                mm(pb[4][:, 0:8],
                   hbuf[:, k * 1024 + tb * 128: k * 1024 + (tb + 1) * 128],
                   wrb[:, k * 8:(k + 1) * 8], k == 0, k == 7,
                   reads=[("h", k, tb // 4), ("wrb",)], writes=[("pb", 4)])
            act(lgt[:, 0:8], pb[4][:, 0:8], AF.Copy, reads=[("pb", 4)], writes=[("lgt",)])
            p.do("V", lambda e: e.reduce_max(lgt[:, 8:9], lgt[:, 0:8], axis=AX.X),
                 reads=[("lgt",)], writes=[("m1",)], hz=True)
            vts(lgt[:, 16:24], lgt[:, 0:8], lgt[:, 8:9], ALU.is_ge,
                reads=[("lgt",), ("m1",)], writes=[("eq1",)], hz=True)
            vstt(lgt[:, 24:32], lgt[:, 16:24], -1e30, lgt[:, 0:8], ALU.mult, ALU.add,
                 reads=[("eq1",), ("lgt",)], writes=[("msk",)], hz=True)
            p.do("V", lambda e: e.reduce_max(lgt[:, 9:10], lgt[:, 24:32], axis=AX.X),
                 reads=[("msk",)], writes=[("m2",)], hz=True)
            vts(lgt[:, 0:8], lgt[:, 24:32], lgt[:, 9:10], ALU.is_ge,
                reads=[("msk",), ("m2",)], writes=[("eq2",), ("lgt",)], hz=True)
            vtt(lgt[:, 10:11], lgt[:, 9:10], lgt[:, 8:9], ALU.subtract,
                reads=[("m1",), ("m2",)], writes=[("d21",)], hz=True)
            act(lgt[:, 11:12], lgt[:, 10:11], AF.Sigmoid,
                reads=[("d21",)], writes=[("s2",)])
            act(lgt[:, 12:13], lgt[:, 10:11], AF.Sigmoid, scale=-1.0,
                reads=[("d21",)], writes=[("s1",)])
            vts(lgt[:, 24:32], lgt[:, 16:24], lgt[:, 12:13], ALU.mult,
                reads=[("eq1",), ("s1",)], writes=[("msk",)], hz=True)
            vstt(lgt[:, 16:24], lgt[:, 0:8], lgt[:, 11:12], lgt[:, 24:32],
                 ALU.mult, ALU.add, reads=[("eq2",), ("lgt",), ("s2",), ("msk",)],
                 writes=[("gt", tb), ("eq1",)], hz=True)
            p.do("T", lambda e: e.transpose(pb[5][0:8, 0:128], lgt[:, 16:24],
                                            idf[:, :]),
                 reads=[("gt", tb), ("eq1",), ("idf",)], writes=[("pb", 5)])
            act(gfmb[0:8, tb * 128:(tb + 1) * 128], pb[5][0:8, 0:128], AF.Copy,
                reads=[("pb", 5)], writes=[("gfmb",)])

        p.barrier()

        for e_i in range(NE):
            if e_i > 0:
                for tc in range(2):
                    mm(pb[4][:, :], selm[:, (e_i - 1) * 128: e_i * 128],
                       gfmb[:, tc * 512:(tc + 1) * 512], True, True,
                       reads=[("gfmb",), ("selm",)], writes=[("pb", 4)])
                    act(arena[:, M_GBC + tc * 512: M_GBC + (tc + 1) * 512],
                        pb[4][:, :], AF.Copy, reads=[("pb", 4)], writes=[("gbc", tc)])
            for q in range(4):
                for k in range(8):
                    sl = k % 2
                    dma("Q", stg(sl), dwfc[e_i, k * 128:(k + 1) * 128,
                                           q * 1024:(q + 1) * 1024],
                        writes=[("stgF", sl)])
                    act(arena[:, M_WFC + k * 1024: M_WFC + (k + 1) * 1024], stg(sl),
                        AF.Copy, reads=[("stgF", sl)], writes=[("wfc", k)])
                for fi in range(8):
                    sl = fi % 2
                    dma("Q", stg(sl), dwpr[e_i, q * 1024 + fi * 128:
                                           q * 1024 + (fi + 1) * 128, :],
                        writes=[("stgF", sl)])
                    act(arena[:, M_WPR + fi * 1024: M_WPR + (fi + 1) * 1024], stg(sl),
                        AF.Copy, reads=[("stgF", sl)], writes=[("wpr", fi)])
                for tc in range(2):
                    for fi in range(8):
                        pi = fi % 2
                        pp = pb[pi]
                        for k in range(8):
                            mm(pp[:, :], arena[:, M_WFC + k * 1024 + fi * 128:
                                               M_WFC + k * 1024 + (fi + 1) * 128],
                               hbuf[:, k * 1024 + tc * 512: k * 1024 + (tc + 1) * 512],
                               k == 0, k == 7,
                               reads=[("wfc", k), ("h", k, tc)], writes=[("pb", pi)])
                        act(arena[:, M_H1 + fi * 512: M_H1 + (fi + 1) * 512],
                            pp[:, :], AF.Gelu_apprx_tanh,
                            bias=bfc9[:, e_i * 32 + q * 8 + fi:
                                      e_i * 32 + q * 8 + fi + 1],
                            reads=[("pb", pi), ("bfc9",)], writes=[("h1", fi)])
                        if e_i > 0:
                            vtt(arena[:, M_H1 + fi * 512: M_H1 + (fi + 1) * 512],
                                arena[:, M_H1 + fi * 512: M_H1 + (fi + 1) * 512],
                                arena[:, M_GBC + tc * 512: M_GBC + (tc + 1) * 512],
                                ALU.mult, reads=[("gbc", tc)], writes=[("h1", fi)])
                    for d in range(8):
                        pi = 2 + d % 2
                        pp = pb[pi]
                        wbias = (e_i == 0 and q == 0)
                        for fi in range(8):
                            mm(pp[:, :], arena[:, M_WPR + fi * 1024 + d * 128:
                                               M_WPR + fi * 1024 + (d + 1) * 128],
                               arena[:, M_H1 + fi * 512: M_H1 + (fi + 1) * 512],
                               fi == 0, (fi == 7 and not wbias),
                               reads=[("wpr", fi), ("h1", fi)], writes=[("pb", pi)])
                        if wbias:
                            mm(pp[:, :],
                               bo_row[0:1, 1024 + d * 128: 1024 + (d + 1) * 128],
                               onest[0:1, 0:512], False, False,
                               reads=[("borow",), ("ones",)], writes=[("pb", pi)])
                            mm(pp[:, :], bprj[:, d * 128:(d + 1) * 128],
                               gfmb[:, tc * 512:(tc + 1) * 512], False, True,
                               reads=[("bprj",), ("gfmb",)], writes=[("pb", pi)])
                        xs = xres[:, d * 1024 + tc * 512: d * 1024 + (tc + 1) * 512]
                        vstt(xs, pp[:, :], GBc(d), xs, ALU.mult, ALU.add,
                             reads=[("pb", pi), ("gax",)], writes=[("x", d)])

        p.barrier()

        for tb in range(8):
            buf = (tb % 2) * 1024
            for d in range(8):
                pi = d % 4
                p.do("T", lambda e, dd=d, b=tb, qq=pi: e.transpose(
                    pb[qq][:, 0:128],
                    xres[:, dd * 1024 + b * 128: dd * 1024 + (b + 1) * 128],
                    idf[:, :]),
                    reads=[("x", d), ("idf",)], writes=[("pb", pi)])
                act(f32s[:, buf + d * 128: buf + (d + 1) * 128],
                    pb[pi][:, 0:128], AF.Copy,
                    reads=[("pb", pi)], writes=[("ob", tb % 2)])
            dma("G", dout[t0 + tb * 128: t0 + (tb + 1) * 128, :],
                f32s[:, buf:buf + 1024], reads=[("ob", tb % 2)],
                writes=[("obd", tb % 2)])

        p.barrier()

    p.realize(nc)
    for cm in reversed(ctxs):
        cm.__exit__(None, None, None)
    return nc


def _prep_inputs(inputs):
    import ml_dtypes
    xi = np.asarray(inputs["L0_attn_x"], dtype=np.float32)
    w_in = np.asarray(inputs["L0_attn_w_in"], dtype=np.float32)
    b_in = np.asarray(inputs["L0_attn_b_in"], dtype=np.float32)
    w_a = np.asarray(inputs["L0_attn_w_a"], dtype=np.float32)
    b_a = np.asarray(inputs["L0_attn_b_a"], dtype=np.float32)
    w_b = np.asarray(inputs["L0_attn_w_b"], dtype=np.float32)
    b_b = np.asarray(inputs["L0_attn_b_b"], dtype=np.float32)
    w_out = np.asarray(inputs["L0_attn_w_out"], dtype=np.float32)
    b_out = np.asarray(inputs["L0_attn_b_out"], dtype=np.float32)
    Wfc_s = np.asarray(inputs["L0_ffn_Wfc_s"], dtype=np.float32)
    bfc_s = np.asarray(inputs["L0_ffn_bfc_s"], dtype=np.float32)
    Wpr_s = np.asarray(inputs["L0_ffn_Wproj_s"], dtype=np.float32)
    bpr_s = np.asarray(inputs["L0_ffn_bproj_s"], dtype=np.float32)
    Wr = np.asarray(inputs["L0_ffn_Wr"], dtype=np.float32)
    Wfc = np.asarray(inputs["L0_ffn_Wfc"], dtype=np.float32)
    bfc = np.asarray(inputs["L0_ffn_bfc"], dtype=np.float32)
    Wpr = np.asarray(inputs["L0_ffn_Wproj"], dtype=np.float32)
    bpr = np.asarray(inputs["L0_ffn_bproj"], dtype=np.float32)
    cc = np.asarray(inputs["L0_c"], dtype=np.float32)
    wmod = np.asarray(inputs["L0_wmod"], dtype=np.float32)

    w_inT = np.ascontiguousarray(w_in.T)
    w_outT = np.ascontiguousarray(w_out.T)
    wpad = np.zeros((NF, E2), dtype=np.float32)
    wpad[:256, :D] = w_a[:, 0, ::-1].T
    wpad[:256, D:] = w_b[:, 0, ::-1].T
    bab = np.ascontiguousarray(np.concatenate([b_a, b_b]).reshape(1, E2))
    WfcA = np.ascontiguousarray(np.concatenate([Wfc_s[None], Wfc], axis=0))
    WprA = np.ascontiguousarray(np.concatenate([Wpr_s[None], Wpr], axis=0))
    bfc9 = np.ascontiguousarray(
        np.concatenate([bfc_s[None], bfc], axis=0).reshape(NE, 32, 128)
        .transpose(2, 0, 1).reshape(128, NE * 32))

    n = np.arange(NF)
    f = np.arange(NF)
    ang = 2.0 * np.pi * np.outer(n, f) / NF
    EfRe = np.cos(ang).astype(np.float32)
    EfIm = (-np.sin(ang)).astype(np.float32)
    tp = np.arange(SEG) + SEG
    angi = 2.0 * np.pi * np.outer(f, tp) / NF
    EiRe = (np.cos(angi) / NF).astype(np.float32)
    EiIm = (-np.sin(angi) / NF).astype(np.float32)
    onesb = np.ones((128, 512), dtype=ml_dtypes.bfloat16)
    idf = np.eye(128, dtype=np.float32)
    selmat = np.zeros((8, 8 * 128), dtype=ml_dtypes.bfloat16)
    for e in range(8):
        selmat[e, e * 128:(e + 1) * 128] = 1.0

    shared = {
        "w_inT": w_inT, "b_in": np.ascontiguousarray(b_in.reshape(1, E2)),
        "w_outT": w_outT, "b_out": np.ascontiguousarray(b_out.reshape(1, D)),
        "wpad": wpad, "b_ab": bab,
        "WfcA": WfcA, "WprA": WprA, "bfc9": bfc9,
        "bprj": np.ascontiguousarray(bpr),
        "bprjs": np.ascontiguousarray(bpr_s.reshape(1, D)),
        "wr": np.ascontiguousarray(Wr), "wmod": np.ascontiguousarray(wmod),
        "EfRe": EfRe, "EfIm": EfIm, "EiRe": EiRe, "EiIm": EiIm,
        "onesb": onesb, "idf": idf, "selm": selmat,
    }
    in_maps = []
    for b in range(8):
        m = dict(shared)
        m["xT"] = np.ascontiguousarray(xi[b].T)
        m["cvec"] = np.ascontiguousarray(cc[b].reshape(8, 128).T)
        in_maps.append(m)
    return in_maps


def kernel(**inputs):
    in_maps = _prep_inputs(inputs)
    nc = build_nc()
    res = run_bass_kernel_spmd(nc, in_maps, core_ids=list(range(8)))
    out = np.stack([np.asarray(r["out"], dtype=np.float32) for r in res.results])
    return out


if __name__ == "__main__":
    nc = build_nc()
    n_ops = {e: len(v) for e, v in {"T": [], "V": [], "S": [], "Q": [], "G": []}.items()}
    print("built ok")
